# version 01
# baseline (speedup 1.0000x reference)
import numpy as np
import ml_dtypes
import concourse.bass as bass
import concourse.mybir as mybir
from concourse.bass_utils import run_bass_kernel_spmd
from concourse.alu_op_type import AluOpType

F32 = mybir.dt.float32
BF16 = mybir.dt.bfloat16
AX = mybir.AxisListType
AF = mybir.ActivationFunctionType
BF = ml_dtypes.bfloat16

B, S, D = 4, 4096, 1024
T = 2048
NT = T // 512
DC = D // 128
DI = 2048
NH, HD = 16, 64
PAD = 16
TP = T + PAD
EPS = 1e-5
NCORES = 8
NEG = -1.0e30


class Prog:
    ENGS = ("tensor", "vector", "scalar", "gpsimd", "sync")

    def __init__(self, nc, sem_handles):
        self.nc = nc
        self.streams = {e: [] for e in self.ENGS}
        self.h = sem_handles
        self.n = {e: 0 for e in self.ENGS}

    def op(self, eng, fn, inc=True):
        if inc:
            self.n[eng] += 1
            h = self.h[eng]
            self.streams[eng].append(lambda e, f=fn, hh=h: f(e).then_inc(hh, 1))
        else:
            self.streams[eng].append(lambda e, f=fn: f(e))
        return self.n[eng]

    def dma(self, eng, out, in_, **kw):
        self.n[eng] += 16
        h = self.h[eng]
        self.streams[eng].append(
            lambda e, o=out, i=in_, hh=h: e.dma_start(out=o, in_=i, **kw).then_inc(hh, 16))
        return self.n[eng]

    def wait(self, eng, src_eng, val=None):
        v = self.n[src_eng] if val is None else val
        if v <= 0:
            return
        h = self.h[src_eng]
        self.streams[eng].append(lambda e, hh=h, vv=v: e.wait_ge(hh, vv))

    def mark(self, eng):
        return self.n[eng]

    def wait_all(self, eng, excl=()):
        for s in self.ENGS:
            if s != eng and s not in excl:
                self.wait(eng, s)

    def barrier(self):
        marks = {e: self.n[e] for e in self.ENGS}
        for e in self.ENGS:
            self.n[e] += 1
            h = self.h[e]
            self.streams[e].append(lambda en, hh=h: en.nop().then_inc(hh, 1))
        for e in self.ENGS:
            for s in self.ENGS:
                if s != e:
                    self.wait(e, s, marks[s] + 1)

    def finalize(self):
        nc = self.nc
        st = self.streams
        with nc.Block() as blk:
            @blk.tensor
            def _(e):
                for f in st["tensor"]:
                    f(e)

            @blk.vector
            def _(e):
                for f in st["vector"]:
                    f(e)

            @blk.scalar
            def _(e):
                for f in st["scalar"]:
                    f(e)

            @blk.gpsimd
            def _(e):
                for f in st["gpsimd"]:
                    f(e)

            @blk.sync
            def _(e):
                for f in st["sync"]:
                    f(e)


def make_prog(nc, stack):
    hs = {}
    for e in Prog.ENGS:
        hs[e] = stack.enter_context(nc.semaphore(f"sem_{e}"))
    return Prog(nc, hs)



def emit_layernorm(p, x_sb, xn_sb, scr, tcs):
    ps_row, ps_row2 = scr["ps_row"], scr["ps_row2"]
    mrow, vrow, rrow = scr["mrow"], scr["vrow"], scr["rrow"]
    mb, rb = scr["mb"], scr["rb"]
    xc, xc2 = scr["xc"], scr["xc2"]
    ones1, onesb = scr["ones1_f32"], scr["onesb"]
    rows_d = scr["ln_rows_dram"]

    for (co, w) in tcs:
        p.wait("tensor", "vector")
        for dc in range(DC):
            p.op("tensor", lambda e, dc=dc, co=co, w=w: e.matmul(
                ps_row[:, :w], ones1[:], x_sb[:, dc * TP + co: dc * TP + co + w],
                start=(dc == 0), stop=(dc == DC - 1)), inc=(dc == DC - 1))
        p.wait("vector", "tensor")
        p.op("vector", lambda e, co=co, w=w: e.tensor_scalar(
            mrow[:, co:co + w], ps_row[:, :w], 1.0 / D, None, op0=AluOpType.mult))
    p.wait("sync", "vector")
    p.dma("sync", rows_d[0:1, :], mrow[:])
    p.wait("sync", "sync")
    p.dma("sync", mb[:], rows_d[0:1, :].to_broadcast([128, TP]))

    for (co, w) in tcs:
        p.wait("gpsimd", "sync")
        p.wait("gpsimd", "tensor")
        for dc in range(DC):
            p.op("gpsimd", lambda e, dc=dc, co=co, w=w: e.tensor_tensor(
                xc[:, dc * 512: dc * 512 + w],
                x_sb[:, dc * TP + co: dc * TP + co + w],
                mb[:, co:co + w], op=AluOpType.subtract))
        p.wait("scalar", "gpsimd")
        for dc in range(DC):
            p.op("scalar", lambda e, dc=dc, w=w: e.activation(
                xc2[:, dc * 512: dc * 512 + w], xc[:, dc * 512: dc * 512 + w],
                AF.Square))
        p.wait("tensor", "scalar")
        p.wait("tensor", "vector")
        for dc in range(DC):
            p.op("tensor", lambda e, dc=dc, w=w: e.matmul(
                ps_row2[:, :w], onesb[:], xc2[:, dc * 512: dc * 512 + w],
                start=(dc == 0), stop=(dc == DC - 1)), inc=(dc == DC - 1))
        p.wait("vector", "tensor")
        p.op("vector", lambda e, co=co, w=w: e.tensor_scalar(
            vrow[:, co:co + w], ps_row2[:, :w], 1.0 / D, EPS,
            op0=AluOpType.mult, op1=AluOpType.add))
    p.wait("scalar", "vector")
    p.op("scalar", lambda e: e.activation(vrow[:], vrow[:], AF.Sqrt))
    p.wait("vector", "scalar")
    p.op("vector", lambda e: e.reciprocal(rrow[:], vrow[:]))
    p.wait("sync", "vector")
    p.dma("sync", rows_d[1:2, :], rrow[:])
    p.wait("sync", "sync")
    p.dma("sync", rb[:], rows_d[1:2, :].to_broadcast([128, TP]))

    for (co, w) in tcs:
        p.wait("gpsimd", "sync")
        p.wait("gpsimd", "vector")
        for dc in range(DC):
            p.op("gpsimd", lambda e, dc=dc, co=co, w=w: e.tensor_tensor(
                xc[:, dc * 512: dc * 512 + w],
                x_sb[:, dc * TP + co: dc * TP + co + w],
                mb[:, co:co + w], op=AluOpType.subtract))
        p.wait("vector", "gpsimd")
        for dc in range(DC):
            p.op("vector", lambda e, dc=dc, co=co, w=w: e.tensor_tensor(
                xn_sb[:, dc * TP + co: dc * TP + co + w],
                xc[:, dc * 512: dc * 512 + w],
                rb[:, co:co + w], op=AluOpType.mult))


def ln_scratch(nc, stack, tag=""):
    sb = lambda name, shp, dt: stack.enter_context(
        nc.sbuf_tensor(name + tag, shp, dt))
    ps = lambda name, shp: stack.enter_context(
        nc.psum_tensor(name + tag, shp, F32))
    scr = {
        "ps_row": ps("lnps1", [1, 512]),
        "ps_row2": ps("lnps2", [1, 512]),
        "mrow": sb("ln_mrow", [1, TP], F32),
        "vrow": sb("ln_vrow", [1, TP], F32),
        "rrow": sb("ln_rrow", [1, TP], F32),
        "mb": sb("ln_mb", [128, TP], F32),
        "rb": sb("ln_rb", [128, TP], F32),
        "xc": sb("ln_xc", [128, DC * 512], BF16),
        "xc2": sb("ln_xc2", [128, DC * 512], BF16),
        "ones1_f32": sb("ln_ones1", [128, 1], F32),
        "onesb": sb("ln_onesb", [128, 1], BF16),
        "ln_rows_dram": nc.dram_tensor("ln_rows" + tag, [2, TP], F32,
                                       kind="Internal").ap(),
    }
    return scr


def emit_const_init(p, scr):
    p.op("gpsimd", lambda e: e.memset(scr["ones1_f32"][:], 1.0))
    p.op("gpsimd", lambda e: e.memset(scr["onesb"][:], 1.0))


STD_TCS = [(PAD + 512 * c, 512) for c in range(NT)]



def build_l1():
    import contextlib
    nc = bass.Bass("TRN2", target_bir_lowering=False)
    xT = nc.dram_tensor("xT", [D, T], F32, kind="ExternalInput").ap()
    w_in = nc.dram_tensor("w_inT", [D, 2 * DI], F32, kind="ExternalInput").ap()
    b_in = nc.dram_tensor("b_in", [128, 2 * DI // 128], F32, kind="ExternalInput").ap()
    w_b = nc.dram_tensor("w_bT", [DI, 16], F32, kind="ExternalInput").ap()
    xp_o = nc.dram_tensor("xp", [DI, T], BF16, kind="ExternalOutput").ap()
    sz_o = nc.dram_tensor("sz", [DI, T], BF16, kind="ExternalOutput").ap()
    b_o = nc.dram_tensor("b_ssm", [16, T], F32, kind="ExternalOutput").ap()

    NO = 2 * DI // 128

    with contextlib.ExitStack() as stack:
        sb = lambda name, shp, dt: stack.enter_context(nc.sbuf_tensor(name, shp, dt))
        pst = lambda name, shp: stack.enter_context(nc.psum_tensor(name, shp, F32))
        x_sb = sb("x_sb", [128, DC * TP], F32)
        xn_sb = sb("xn_sb", [128, DC * TP], BF16)
        biases = sb("biases", [128, NO], F32)
        wt = [sb("wt0", [128, DC * 128], BF16), sb("wt1", [128, DC * 128], BF16)]
        wbs = sb("wbs", [128, 16 * 16], BF16)
        out_sb = sb("out_sb", [128, 2 * T], BF16)
        b_sb = sb("b_sb", [16, T], F32)
        ps = [pst("ps_a", [128, 512]), pst("ps_b", [128, 512])]
        ps_bp = pst("ps_bp", [16, NT * 512])
        scr = ln_scratch(nc, stack)
        p = make_prog(nc, stack)

        emit_const_init(p, scr)
        for dc in range(DC):
            p.dma("gpsimd", x_sb[:, dc * TP + PAD: dc * TP + PAD + T],
                  xT[dc * 128:(dc + 1) * 128, :])
        p.dma("gpsimd", biases[:], b_in[:])
        p.dma("gpsimd", wbs[:].rearrange("p (c s) -> p c s", c=16),
              w_b.rearrange("(c p) s -> p c s", p=128))
        p.wait("gpsimd", "gpsimd")
        p.op("gpsimd", lambda e: e.tensor_scalar(
            wbs[:], wbs[:], 0.1, None, op0=AluOpType.mult))
        for e in ("tensor", "vector", "scalar", "sync"):
            p.wait(e, "gpsimd")

        emit_layernorm(p, x_sb, xn_sb, scr, STD_TCS)

        evac_done = {}
        odma_done = {}
        p.wait("gpsimd", "vector")
        for j in range(NO):
            w = wt[j % 2]
            if j >= 2:
                p.wait("gpsimd", "tensor")
            p.dma("gpsimd", w[:].rearrange("p (c o) -> p c o", c=DC),
                  w_in[:, j * 128:(j + 1) * 128]
                  .rearrange("(c p) o -> p c o", p=128))
            wload = p.mark("gpsimd")
            half, oc = j // 16, j % 16
            for tc in range(NT):
                idx = j * NT + tc
                pst_ = ps[idx % 2]
                p.wait("tensor", "gpsimd", wload)
                if idx >= 2:
                    p.wait("tensor", "scalar", evac_done[idx - 2])
                if tc == 0:
                    p.wait("tensor", "vector")
                for dc in range(DC):
                    p.op("tensor", lambda e, w=w, dc=dc, tc=tc, pst_=pst_: e.matmul(
                        pst_[:], w[:, dc * 128:(dc + 1) * 128],
                        xn_sb[:, dc * TP + PAD + 512 * tc: dc * TP + PAD + 512 * tc + 512],
                        start=(dc == 0), stop=(dc == DC - 1)), inc=(dc == DC - 1))
                p.wait("scalar", "tensor")
                if idx >= 2 * NT:
                    p.wait("scalar", "sync", odma_done.get(j - 2, 0))
                slot = (j % 2) * T + 512 * tc
                evac_done[idx] = p.op(
                    "scalar", lambda e, pst_=pst_, slot=slot, j=j: e.activation(
                        out_sb[:, slot: slot + 512], pst_[:], AF.Silu,
                        bias=biases[:, j:j + 1], scale=1.0))
                if half == 0:
                    p.wait("tensor", "scalar", evac_done[idx])
                    p.op("tensor", lambda e, oc=oc, tc=tc, slot=slot: e.matmul(
                        ps_bp[:, tc * 512:(tc + 1) * 512],
                        wbs[:, oc * 16: oc * 16 + 16],
                        out_sb[:, slot: slot + 512],
                        start=(oc == 0), stop=(oc == 15)), inc=(oc == 15))
                    if oc == 15:
                        p.wait("vector", "tensor")
                        p.op("vector", lambda e, tc=tc: e.tensor_copy(
                            b_sb[:, 512 * tc: 512 * tc + 512],
                            ps_bp[:, tc * 512:(tc + 1) * 512]))
                p.wait("sync", "scalar", evac_done[idx])
                dst = xp_o if half == 0 else sz_o
                odma_done[j] = p.dma(
                    "sync", dst[oc * 128:(oc + 1) * 128, 512 * tc:512 * tc + 512],
                    out_sb[:, slot: slot + 512])

        p.wait("sync", "vector")
        p.dma("sync", b_o[:], b_sb[:])
        p.wait("sync", "sync")
        p.finalize()
    return nc



def emit_proj_stream(p, w_d, wt, ps_pool, n_ec, o_chunks, rhs_ap, evac,
                     tcs=None):
    tcs = tcs if tcs is not None else list(range(NT))
    evac_done = {}
    mm_done = {}
    for jj, j in enumerate(o_chunks):
        w = wt[jj % 2]
        if jj >= 2:
            p.wait("gpsimd", "tensor", mm_done[jj - 2])
        else:
            p.wait("gpsimd", "tensor")
        p.dma("gpsimd", w[:].rearrange("p (c o) -> p c o", c=n_ec),
              w_d[:, j * 128:(j + 1) * 128].rearrange("(c p) o -> p c o", p=128))
        wload = p.mark("gpsimd")
        for ti, tc in enumerate(tcs):
            idx = jj * len(tcs) + ti
            ps = ps_pool[idx % len(ps_pool)]
            p.wait("tensor", "gpsimd", wload)
            if idx >= len(ps_pool):
                eng, mk = evac_done[idx - len(ps_pool)]
                p.wait("tensor", eng, mk)
            for ec in range(n_ec):
                p.op("tensor", lambda e, w=w, ec=ec, tc=tc, ps=ps: e.matmul(
                    ps[:], w[:, ec * 128:(ec + 1) * 128], rhs_ap(ec, tc),
                    start=(ec == 0), stop=(ec == n_ec - 1)), inc=(ec == n_ec - 1))
            evac_done[idx] = evac(j, tc, ps)
        mm_done[jj] = p.mark("tensor")



def emit_ffn(p, nc, stack, x_sb, xn_sb, ln_scr, wr_d, wk_d, wv_d, mur, muk,
             tag=""):
    sb = lambda name, shp, dt: stack.enter_context(nc.sbuf_tensor(name + tag, shp, dt))
    pst = lambda name, shp: stack.enter_context(nc.psum_tensor(name + tag, shp, F32))
    omr = sb("f_omr", [128, DC], F32)
    omk = sb("f_omk", [128, DC], F32)
    t1 = sb("f_t1", [128, 512], BF16)
    xr = sb("f_xr", [128, DC * 512], BF16)
    xk = sb("f_xk", [128, DC * 512], BF16)
    r_sb = sb("f_r", [128, DC * 512], BF16)
    ka = sb("f_ka", [128, DC * 512], BF16)
    krelu = sb("f_krelu", [128, 512], BF16)
    t3 = sb("f_t3", [128, 512], F32)
    wt = [sb("f_wt0", [128, DC * 128], BF16), sb("f_wt1", [128, DC * 128], BF16)]
    ps_pool = [pst("f_ps0", [128, 512]), pst("f_ps1", [128, 512])]

    p.op("gpsimd", lambda e: e.tensor_scalar(
        omr[:], mur[:], -1.0, 1.0, op0=AluOpType.mult, op1=AluOpType.add))
    p.op("gpsimd", lambda e: e.tensor_scalar(
        omk[:], muk[:], -1.0, 1.0, op0=AluOpType.mult, op1=AluOpType.add))

    emit_layernorm(p, x_sb, xn_sb, ln_scr, [(PAD - 1, 1)] + STD_TCS)

    for tc in range(NT):
        co = PAD + 512 * tc
        for dc in range(DC):
            xn_c = xn_sb[:, dc * TP + co: dc * TP + co + 512]
            xn_p = xn_sb[:, dc * TP + co - 1: dc * TP + co + 511]
            for (dst, mu, om) in ((xr, mur, omr), (xk, muk, omk)):
                p.wait("gpsimd", "vector")
                p.op("gpsimd", lambda e, xn_p=xn_p, om=om, dc=dc: e.tensor_scalar(
                    t1[:], xn_p, om[:, dc:dc + 1], None, op0=AluOpType.mult))
                p.wait("vector", "gpsimd")
                p.op("vector", lambda e, dst=dst, xn_c=xn_c, mu=mu, dc=dc:
                     e.scalar_tensor_tensor(
                         dst[:, dc * 512:(dc + 1) * 512], xn_c, mu[:, dc:dc + 1],
                         t1[:], op0=AluOpType.mult, op1=AluOpType.add))
        mix_done = p.mark("vector")

        def evac_r(j, _tc, ps):
            p.wait("scalar", "tensor")
            mk = p.op("scalar", lambda e, ps=ps, j=j: e.activation(
                r_sb[:, j * 512:(j + 1) * 512], ps[:], AF.Sigmoid))
            return ("scalar", mk)

        p.wait("tensor", "vector", mix_done)
        emit_proj_stream(p, wr_d, wt, ps_pool, DC, list(range(DC)),
                         lambda ec, _t: xr[:, ec * 512:(ec + 1) * 512],
                         evac_r, tcs=[tc])

        def evac_k(j, _tc, ps):
            p.wait("scalar", "tensor")
            p.op("scalar", lambda e, ps=ps: e.activation(krelu[:], ps[:], AF.Relu))
            mk = p.op("scalar", lambda e, j=j: e.activation(
                ka[:, j * 512:(j + 1) * 512], krelu[:], AF.Square))
            return ("scalar", mk)

        emit_proj_stream(p, wk_d, wt, ps_pool, DC, list(range(DC)),
                         lambda ec, _t: xk[:, ec * 512:(ec + 1) * 512],
                         evac_k, tcs=[tc])

        def evac_v(j, _tc, ps):
            p.wait("vector", "tensor")
            p.op("vector", lambda e, ps=ps, j=j: e.scalar_tensor_tensor(
                t3[:], ps[:], 1.0, r_sb[:, j * 512:(j + 1) * 512],
                op0=AluOpType.mult, op1=AluOpType.mult))
            co2 = j * TP + PAD + 512 * tc
            mk = p.op("vector", lambda e, co2=co2: e.tensor_tensor(
                x_sb[:, co2: co2 + 512], x_sb[:, co2: co2 + 512], t3[:],
                op=AluOpType.add))
            return ("vector", mk)

        emit_proj_stream(p, wv_d, wt, ps_pool, DC, list(range(DC)),
                         lambda ec, _t: ka[:, ec * 512:(ec + 1) * 512],
                         evac_v, tcs=[tc])



def build_l2(debug=False):
    import contextlib
    nc = bass.Bass("TRN2", target_bir_lowering=False)
    xT = nc.dram_tensor("xT", [D, T], F32, kind="ExternalInput").ap()
    xp_d = nc.dram_tensor("xp", [DI, T], BF16, kind="ExternalInput").ap()
    sz_d = nc.dram_tensor("sz", [DI, T], BF16, kind="ExternalInput").ap()
    bx_d = nc.dram_tensor("b_ext", [16, 2 * T], F32, kind="ExternalInput").ap()
    wc_d = nc.dram_tensor("w_cT", [16, DI], F32, kind="ExternalInput").ap()
    wo_d = nc.dram_tensor("w_outT", [DI, D], F32, kind="ExternalInput").ap()
    bo_d = nc.dram_tensor("b_out", [128, DC], F32, kind="ExternalInput").ap()
    dv_d = nc.dram_tensor("d_vec", [128, 16], F32, kind="ExternalInput").ap()
    x1_o = nc.dram_tensor("x1T", [D, T], F32, kind="ExternalOutput").ap()
    if debug:
        hs_o = nc.dram_tensor("hs_dbg", [16, 2 * T], F32, kind="ExternalOutput").ap()
        y_o = nc.dram_tensor("y_dbg", [DI, 512], BF16, kind="ExternalOutput").ap()

    EC = DI // 128

    with contextlib.ExitStack() as stack:
        sb = lambda name, shp, dt: stack.enter_context(nc.sbuf_tensor(name, shp, dt))
        pst = lambda name, shp: stack.enter_context(nc.psum_tensor(name, shp, F32))
        x_sb = sb("x_sb", [128, DC * TP], F32)
        b_sb = sb("b_sb", [16, 2 * T], F32)
        d09 = sb("d09", [16, 2 * T], F32)
        hs = sb("hs", [16, 2 * T], F32)
        hsb = sb("hsb", [16, T], BF16)
        wc = sb("wc", [16, DI], BF16)
        wo = sb("wo", [128, EC * D], BF16)
        bo = sb("bo", [128, DC], F32)
        dv = sb("dv", [128, 16], F32)
        xp_t = [sb("xp_t0", [128, 512], BF16), sb("xp_t1", [128, 512], BF16)]
        sz_t = [sb("sz_t0", [128, 512], BF16), sb("sz_t1", [128, 512], BF16)]
        t1 = [sb("t1_0", [128, 512], BF16), sb("t1_1", [128, 512], BF16)]
        t2 = [sb("t2_0", [128, 512], F32), sb("t2_1", [128, 512], F32)]
        y_sb = sb("y_sb", [128, EC * 512], BF16)
        ps = [pst("ps_a", [128, 512]), pst("ps_b", [128, 512])]
        p = make_prog(nc, stack)

        for dc in range(DC):
            p.dma("gpsimd", x_sb[:, dc * TP + PAD: dc * TP + PAD + T],
                  xT[dc * 128:(dc + 1) * 128, :])
        p.dma("gpsimd", b_sb[:], bx_d[:])
        p.dma("gpsimd", wc[:], wc_d[:])
        p.dma("gpsimd", wo[:].rearrange("p (c o) -> p c o", c=EC),
              wo_d.rearrange("(c p) o -> p c o", p=128))
        p.dma("gpsimd", bo[:], bo_d[:])
        p.dma("gpsimd", dv[:], dv_d[:])
        p.op("gpsimd", lambda e: e.memset(d09[:], 0.9))

        p.wait("vector", "gpsimd")
        p.op("vector", lambda e: e.tensor_tensor_scan(
            hs[:], d09[:], b_sb[:], 0.0, op0=AluOpType.mult, op1=AluOpType.add))
        p.op("vector", lambda e: e.tensor_copy(hsb[:], hs[:, T:]))
        hs_done = p.mark("vector")

        for tc in range(NT):
            for ec in range(EC):
                k = tc * EC + ec
                pb = k % 2
                pst_ = ps[pb]
                if k >= 2:
                    p.wait("gpsimd", "gpsimd")
                    p.wait("gpsimd", "vector")
                p.dma("gpsimd", xp_t[pb][:],
                      xp_d[ec * 128:(ec + 1) * 128, tc * 512: tc * 512 + 512])
                p.dma("gpsimd", sz_t[pb][:],
                      sz_d[ec * 128:(ec + 1) * 128, tc * 512: tc * 512 + 512])
                ld = p.mark("gpsimd")
                p.wait("gpsimd", "gpsimd", ld)
                p.op("gpsimd", lambda e, pb=pb, ec=ec: e.tensor_scalar(
                    t1[pb][:], xp_t[pb][:], dv[:, ec:ec + 1], None,
                    op0=AluOpType.mult))
                t1_done = p.mark("gpsimd")
                p.wait("tensor", "vector", hs_done)
                if k >= 2:
                    p.wait("tensor", "vector")
                p.op("tensor", lambda e, pst_=pst_, ec=ec, tc=tc: e.matmul(
                    pst_[:], wc[:, ec * 128:(ec + 1) * 128],
                    hsb[:, tc * 512: tc * 512 + 512], start=True, stop=True))
                p.wait("vector", "tensor")
                p.wait("vector", "gpsimd", t1_done)
                p.op("vector", lambda e, pst_=pst_, pb=pb: e.scalar_tensor_tensor(
                    t2[pb][:], pst_[:], 1.0, t1[pb][:],
                    op0=AluOpType.mult, op1=AluOpType.add))
                p.op("vector", lambda e, pb=pb, ec=ec: e.tensor_tensor(
                    y_sb[:, ec * 512:(ec + 1) * 512], t2[pb][:], sz_t[pb][:],
                    op=AluOpType.mult))
            y_done = p.mark("vector")
            if debug and tc == 0:
                p.wait("sync", "vector", y_done)
                p.dma("sync", y_o.rearrange("(c p) t -> p c t", p=128),
                      y_sb[:].rearrange("p (c t) -> p c t", c=EC))

            for oc in range(DC):
                k2 = tc * DC + oc
                pst_ = ps[k2 % 2]
                p.wait("tensor", "vector", y_done)
                p.wait("tensor", "vector")
                for ec in range(EC):
                    p.op("tensor", lambda e, pst_=pst_, ec=ec, oc=oc: e.matmul(
                        pst_[:], wo[:, ec * D + oc * 128: ec * D + oc * 128 + 128],
                        y_sb[:, ec * 512:(ec + 1) * 512],
                        start=(ec == 0), stop=(ec == EC - 1)), inc=(ec == EC - 1))
                p.wait("vector", "tensor")
                co = oc * TP + PAD + 512 * tc
                p.op("vector", lambda e, pst_=pst_, oc=oc, co=co: e.scalar_tensor_tensor(
                    x_sb[:, co: co + 512], pst_[:], bo[:, oc:oc + 1],
                    x_sb[:, co: co + 512],
                    op0=AluOpType.add, op1=AluOpType.add))

        p.wait("sync", "vector")
        for dc in range(DC):
            p.dma("sync", x1_o[dc * 128:(dc + 1) * 128, :],
                  x_sb[:, dc * TP + PAD: dc * TP + PAD + T])
        if debug:
            p.dma("sync", hs_o[:], hs[:])
        p.wait("sync", "sync")
        p.finalize()
    return nc



def build_l3():
    import contextlib
    nc = bass.Bass("TRN2", target_bir_lowering=False)
    xT = nc.dram_tensor("xT", [D, T], F32, kind="ExternalInput").ap()
    halo = nc.dram_tensor("halo", [D, 1], F32, kind="ExternalInput").ap()
    wr_d = nc.dram_tensor("w_rT", [D, D], F32, kind="ExternalInput").ap()
    wk_d = nc.dram_tensor("w_kT", [D, D], F32, kind="ExternalInput").ap()
    wv_d = nc.dram_tensor("w_vT", [D, D], F32, kind="ExternalInput").ap()
    mur_d = nc.dram_tensor("mu_r", [128, DC], F32, kind="ExternalInput").ap()
    muk_d = nc.dram_tensor("mu_k", [128, DC], F32, kind="ExternalInput").ap()
    wq_d = nc.dram_tensor("w_kqvT", [D, 3 * D], F32, kind="ExternalInput").ap()
    bq_d = nc.dram_tensor("b_kqv", [128, 3 * DC], F32, kind="ExternalInput").ap()
    x2_o = nc.dram_tensor("x2T", [D, T], F32, kind="ExternalOutput").ap()
    kT_o = nc.dram_tensor("kT", [D, T], BF16, kind="ExternalOutput").ap()
    qT_o = nc.dram_tensor("qT", [D, T], BF16, kind="ExternalOutput").ap()
    vT_o = nc.dram_tensor("vT", [D, T], BF16, kind="ExternalOutput").ap()

    with contextlib.ExitStack() as stack:
        sb = lambda name, shp, dt: stack.enter_context(nc.sbuf_tensor(name, shp, dt))
        pst = lambda name, shp: stack.enter_context(nc.psum_tensor(name, shp, F32))
        x_sb = sb("x_sb", [128, DC * TP], F32)
        xn_sb = sb("xn_sb", [128, DC * TP], BF16)
        mur = sb("mur", [128, DC], F32)
        muk = sb("muk", [128, DC], F32)
        bq = sb("bq", [128, 3 * DC], F32)
        bq8 = sb("bq8", [128, DC], F32)
        out_sb = sb("out_sb", [128, 2 * T], BF16)
        scr = ln_scratch(nc, stack)
        p = make_prog(nc, stack)

        emit_const_init(p, scr)
        for dc in range(DC):
            p.dma("gpsimd", x_sb[:, dc * TP + PAD: dc * TP + PAD + T],
                  xT[dc * 128:(dc + 1) * 128, :])
            p.dma("gpsimd", x_sb[:, dc * TP + PAD - 1: dc * TP + PAD],
                  halo[dc * 128:(dc + 1) * 128, :])
        p.dma("gpsimd", mur[:], mur_d[:])
        p.dma("gpsimd", muk[:], muk_d[:])
        p.dma("gpsimd", bq[:], bq_d[:])
        p.wait("gpsimd", "gpsimd")
        p.op("gpsimd", lambda e: e.tensor_scalar(
            bq8[:], bq[:, DC:2 * DC], 0.125, None, op0=AluOpType.mult))
        for e in ("tensor", "vector", "scalar", "sync"):
            p.wait(e, "gpsimd")

        emit_ffn(p, nc, stack, x_sb, xn_sb, scr, wr_d, wk_d, wv_d, mur, muk)

        emit_layernorm(p, x_sb, xn_sb, scr, STD_TCS)

        wt = [sb("q_wt0", [128, DC * 128], BF16), sb("q_wt1", [128, DC * 128], BF16)]
        ps_pool = [pst("q_ps0", [128, 512]), pst("q_ps1", [128, 512])]
        odma = {}

        def evac_qkv(j, tc, ps):
            region = j // DC
            oc = j % DC
            slot = (j % 2) * T + 512 * tc
            p.wait("vector", "tensor")
            if j >= 2:
                p.wait("vector", "sync", odma.get((j - 2, tc), 0))
            if region == 1:
                mk = p.op("vector", lambda e, ps=ps, oc=oc, slot=slot: e.tensor_scalar(
                    out_sb[:, slot:slot + 512], ps[:], 0.125, bq8[:, oc:oc + 1],
                    op0=AluOpType.mult, op1=AluOpType.add))
            else:
                col = region * DC + oc
                mk = p.op("vector", lambda e, ps=ps, col=col, slot=slot: e.tensor_scalar(
                    out_sb[:, slot:slot + 512], ps[:], bq[:, col:col + 1], None,
                    op0=AluOpType.add))
            dst = (kT_o, qT_o, vT_o)[region]
            p.wait("sync", "vector", mk)
            odma[(j, tc)] = p.dma(
                "sync", dst[oc * 128:(oc + 1) * 128, 512 * tc:512 * tc + 512],
                out_sb[:, slot:slot + 512])
            return ("vector", mk)

        emit_proj_stream(p, wq_d, wt, ps_pool, DC, list(range(3 * DC)),
                         lambda ec, tc: xn_sb[:, ec * TP + PAD + 512 * tc:
                                              ec * TP + PAD + 512 * tc + 512],
                         evac_qkv)

        p.wait("sync", "vector")
        for dc in range(DC):
            p.dma("sync", x2_o[dc * 128:(dc + 1) * 128, :],
                  x_sb[:, dc * TP + PAD: dc * TP + PAD + T])
        p.wait("sync", "sync")
        p.finalize()
    return nc



def kblocks_for_qtile(t):
    kt = T + 128 * (t + 1)
    out = []
    b = 0
    while b < kt:
        out.append((b, min(512, kt - b)))
        b += 512
    return out


def build_l4(debug=False):
    import contextlib
    nc = bass.Bass("TRN2", target_bir_lowering=False)
    x2_d = nc.dram_tensor("x2T", [D, T], F32, kind="ExternalInput").ap()
    kT_d = nc.dram_tensor("kT_loc", [D, 2 * T], BF16, kind="ExternalInput").ap()
    qT_d = nc.dram_tensor("qT8", [D, T], BF16, kind="ExternalInput").ap()
    v_d = nc.dram_tensor("vtok", [2 * T, D], BF16, kind="ExternalInput").ap()
    khi_d = nc.dram_tensor("kpos_hi", [1, 2 * T], F32, kind="ExternalInput").ap()
    klo_d = nc.dram_tensor("kpos_lo", [1, 2 * T], F32, kind="ExternalInput").ap()
    slo_d = nc.dram_tensor("slopes_b", [128, NH], F32, kind="ExternalInput").ap()
    wp_d = nc.dram_tensor("w_proj", [D, D], F32, kind="ExternalInput").ap()
    bp_d = nc.dram_tensor("b_proj", [128, DC], F32, kind="ExternalInput").ap()
    x3_o = nc.dram_tensor("x3T", [D, T], F32, kind="ExternalOutput").ap()
    drow_d = nc.dram_tensor("drow_scr", [1, T], F32, kind="Internal").ap()
    ones_d = nc.dram_tensor("ones_row", [1, 2 * T], F32, kind="ExternalInput").ap()
    zero_d = nc.dram_tensor("zero_row", [1, T], F32, kind="ExternalInput").ap()
    if debug:
        mp_o = nc.dram_tensor("mpack_dbg", [128, 16], F32, kind="ExternalOutput").ap()
        r66_o = nc.dram_tensor("row66_dbg", [1, T], BF16, kind="ExternalOutput").ap()
        den_o = nc.dram_tensor("den_dbg", [1, T], F32, kind="ExternalOutput").ap()
        pt_o = nc.dram_tensor("pt_dbg", [128, 512], BF16, kind="ExternalOutput").ap()
        mA_o = nc.dram_tensor("maskA_dbg", [128, 128], F32, kind="ExternalOutput").ap()
        mB_o = nc.dram_tensor("maskB_dbg", [128, 128], F32, kind="ExternalOutput").ap()
        ptd_o = nc.dram_tensor("ptdiag_dbg", [128, 512], BF16, kind="ExternalOutput").ap()
        psA_o = nc.dram_tensor("psA_dbg", [128, 128], F32, kind="ExternalOutput").ap()
        part_o = nc.dram_tensor("part_dbg", [128, 16], F32, kind="ExternalOutput").ap()

    NKT = 2 * T // 128
    CON = 67

    with contextlib.ExitStack() as stack:
        sb = lambda n, s, d: stack.enter_context(nc.sbuf_tensor(n, s, d))
        pst = lambda n, s: stack.enter_context(nc.psum_tensor(n, s, F32))
        p = make_prog(nc, stack)

        y_sb = sb("y_sb", [128, DC * T], BF16)
        slopes = sb("slopes", [128, NH], F32)
        ident = sb("ident", [128, 128], F32)
        iota_t = sb("iota_t", [128, 128], F32)
        maskA = sb("maskA", [128, 128], F32)
        maskB = sb("maskB", [128, 128], F32)
        dbgtile = sb("dbgtile", [128, 128], F32)

        p.dma("gpsimd", slopes[:], slo_d[:])
        p.op("gpsimd", lambda e: e.iota(iota_t[:], pattern=[[-1, 128]], base=0,
                                        channel_multiplier=1,
                                        allow_small_or_imprecise_dtypes=True))
        p.wait("vector", "gpsimd")
        p.op("vector", lambda e: e.tensor_scalar(
            maskA[:], iota_t[:], 0.0, 1e28, op0=AluOpType.min,
            op1=AluOpType.mult))
        p.op("vector", lambda e: e.tensor_scalar(
            maskB[:], iota_t[:], 0.0, -1e28, op0=AluOpType.max,
            op1=AluOpType.mult))
        p.op("vector", lambda e: e.tensor_scalar(
            ident[:], iota_t[:], -1.0, None, op0=AluOpType.mult))
        p.op("vector", lambda e: e.tensor_tensor(
            ident[:], ident[:], iota_t[:], op=AluOpType.max))
        p.op("vector", lambda e: e.tensor_scalar(
            ident[:], ident[:], -1.0, 1.0, op0=AluOpType.mult, op1=AluOpType.add))
        p.op("vector", lambda e: e.tensor_scalar(
            ident[:], ident[:], 0.0, None, op0=AluOpType.max))
        p.wait("gpsimd", "vector")

        with contextlib.ExitStack() as attn_stack:
            asb = lambda n, s, d: attn_stack.enter_context(nc.sbuf_tensor(n, s, d))
            apst = lambda n, s: attn_stack.enter_context(nc.psum_tensor(n, s, F32))
            kaug = [asb("kaug0", [CON, 2 * T], BF16), asb("kaug1", [CON, 2 * T], BF16)]
            qaug = [asb("qaug0", [CON, T], BF16), asb("qaug1", [CON, T], BF16)]
            vaug = [asb(f"vaug{h}", [128, NKT * 65], BF16) for h in range(NH)]
            pt_sb = asb("pt_sb", [128, NKT * 512], BF16)
            part = asb("part", [128, 16], F32)
            mpack = asb("mpack", [128, 16], F32)
            mT = asb("mT", [16, 128], BF16)
            dgather = asb("dgather", [1, T], F32)
            recrow = asb("recrow", [1, T], F32)
            rb64 = asb("rb64", [64, T], F32)
            yh = asb("yh", [64, 512], BF16)
            ps_s = [apst("ps_s0", [128, 512]), apst("ps_s1", [128, 512])]
            ps_av = [apst(f"ps_av{c}", [65, 512]) for c in range(NT)]
            ps_tr = apst("ps_tr", [16, 128])

            for h in range(NH):
                p.op("gpsimd", lambda e, h=h: e.memset(vaug[h][:], 1.0))
            p.wait("gpsimd", "gpsimd")
            for h in range(NH):
                p.dma("gpsimd",
                      vaug[h][:].rearrange("p (j d) -> p j d", j=NKT)[:, :, 0:64],
                      v_d[:, 64 * h: 64 * h + 64]
                      .rearrange("(j p) d -> p j d", p=128))

            for kb in kaug:
                p.dma("gpsimd", kb[64:65, :], khi_d[:])
                p.dma("gpsimd", kb[65:66, :], klo_d[:])
                p.dma("gpsimd", kb[66:67, :], ones_d[:])
            p.wait("gpsimd", "gpsimd")

            stg_free = {}
            div_done = {}
            for h in range(NH):
                par = h % 2
                kb, qb = kaug[par], qaug[par]
                if h >= 2:
                    eng, mk = stg_free[par]
                    p.wait("gpsimd", eng, mk)
                p.dma("gpsimd", kb[0:64, :], kT_d[64 * h: 64 * h + 64, :])
                p.dma("gpsimd", qb[0:64, :], qT_d[64 * h: 64 * h + 64, :])
                p.op("gpsimd", lambda e, qb=qb: e.memset(qb[64:66, :], 1.0))
                p.dma("gpsimd", qb[66:67, :], zero_d[:])
                p.wait("gpsimd", "gpsimd")
                p.op("gpsimd", lambda e, qb=qb, h=h: e.tensor_scalar(
                    qb[64:66, :], qb[64:66, :], slopes[64:66, h:h + 1], None,
                    op0=AluOpType.mult))
                stg_ready = p.mark("gpsimd")

                for t in range(16):
                    blocks = kblocks_for_qtile(t)
                    nb = len(blocks)
                    for bi, (b0, w) in enumerate(blocks):
                        ps = ps_s[bi % 2]
                        p.wait("tensor", "gpsimd", stg_ready)
                        p.wait("tensor", "vector")
                        p.op("tensor", lambda e, ps=ps, qb=qb, kb=kb, t=t,
                             b0=b0, w=w: e.matmul(
                                 ps[:, :w], qb[:, 128 * t: 128 * t + 128],
                                 kb[:, b0: b0 + w], start=True, stop=True))
                        if bi == nb - 1:
                            p.wait("vector", "tensor")
                            p.op("vector", lambda e, ps=ps, w=w: e.tensor_tensor(
                                ps[:, w - 128: w], ps[:, w - 128: w], maskA[:],
                                op=AluOpType.add))
                            if debug and h == 0 and t == 0:
                                p.op("vector", lambda e, ps=ps, w=w: e.tensor_copy(
                                    dbgtile[:], ps[:, w - 128:w]))
                                p.wait("sync", "vector")
                                p.dma("sync", psA_o[:], dbgtile[:])
                        p.wait("vector", "tensor")
                        p.op("vector", lambda e, ps=ps, w=w, bi=bi: e.tensor_reduce(
                            part[:, bi:bi + 1], ps[:, :w], axis=AX.X,
                            op=AluOpType.max))
                    p.wait("vector", "vector")
                    p.op("vector", lambda e, t=t, nb=nb: e.tensor_reduce(
                        mpack[:, t:t + 1], part[:, :nb], axis=AX.X,
                        op=AluOpType.max, negate=True))
                    if debug and h == 0 and t == 0:
                        p.wait("sync", "vector")
                        p.dma("sync", part_o[:], part[:])
                m_done = p.mark("vector")
                p.wait("tensor", "vector", m_done)
                p.op("tensor", lambda e: e.transpose(ps_tr[:], mpack[:], ident[:]))
                p.wait("vector", "tensor")
                p.op("vector", lambda e: e.tensor_copy(mT[:], ps_tr[:]))
                p.wait("sync", "vector")
                p.dma("sync", qb[66:67, :].rearrange("p (t c) -> p t c", t=16),
                      mT[:])
                row66 = p.mark("sync")
                if debug and h == 0:
                    p.dma("sync", mp_o[:], mpack[:])
                    p.wait("sync", "sync")
                    p.dma("sync", r66_o[:], qb[66:67, :])

                for c in range(NT):
                    if h >= 2:
                        p.wait("tensor", "vector", div_done.get((par, c), 0))
                    nkt = 16 + 4 * (c + 1)
                    for j in range(nkt):
                        qoff = 128 * (j - 16 - 4 * c) if j - 16 >= 4 * c else 0
                        wq = 512 - qoff
                        ps = ps_s[j % 2]
                        p.wait("tensor", "sync", row66)
                        p.wait("tensor", "scalar")
                        p.op("tensor", lambda e, ps=ps, kb=kb, qb=qb, j=j, c=c,
                             qoff=qoff, wq=wq: e.matmul(
                                 ps[:, qoff: qoff + wq],
                                 kb[:, 128 * j: 128 * j + 128],
                                 qb[:, 512 * c + qoff: 512 * c + qoff + wq],
                                 start=True, stop=True))
                        diag = (j >= 16 and (j - 16) - 4 * c in range(0, 4)
                                and (j - 16) >= 4 * c)
                        if diag:
                            p.wait("vector", "tensor")
                            p.op("vector", lambda e, ps=ps, qoff=qoff: e.tensor_tensor(
                                ps[:, qoff: qoff + 128], ps[:, qoff: qoff + 128],
                                maskB[:], op=AluOpType.add))
                            p.wait("scalar", "vector")
                        else:
                            p.wait("scalar", "tensor")
                        p.op("scalar", lambda e, ps=ps, j=j, qoff=qoff, wq=wq:
                             e.activation(pt_sb[:, 512 * j + qoff: 512 * j + qoff + wq],
                                          ps[:, qoff: qoff + wq], AF.Exp))
                    av = ps_av[c]
                    for j in range(nkt):
                        qoff = 128 * (j - 16 - 4 * c) if j - 16 >= 4 * c else 0
                        wq = 512 - qoff
                        p.wait("tensor", "scalar")
                        p.op("tensor", lambda e, av=av, h=h, j=j, qoff=qoff, wq=wq:
                             e.matmul(av[:, qoff: qoff + wq],
                                      vaug[h][:, 65 * j: 65 * j + 65],
                                      pt_sb[:, 512 * j + qoff: 512 * j + qoff + wq],
                                      start=(j == 0), stop=(j == nkt - 1)))
                    p.wait("scalar", "tensor")
                    p.op("scalar", lambda e, av=av, c=c: e.copy(
                        dgather[:, 512 * c: 512 * c + 512], av[64:65, :]))
                p.wait("vector", "scalar")
                if debug and h == 0:
                    p.wait("sync", "scalar")
                    p.dma("sync", den_o[:], dgather[:])
                    p.dma("sync", pt_o[:], pt_sb[:, 0:512])
                    p.dma("sync", ptd_o[:], pt_sb[:, 16 * 512:17 * 512])
                    p.dma("sync", mA_o[:], maskA[:])
                    p.dma("sync", mB_o[:], maskB[:])
                p.op("vector", lambda e: e.reciprocal(recrow[:], dgather[:]))
                p.wait("sync", "vector")
                p.dma("sync", drow_d[:], recrow[:])
                p.wait("sync", "sync")
                p.dma("sync", rb64[:], drow_d.to_broadcast([64, T]))
                rb_ready = p.mark("sync")
                for c in range(NT):
                    av = ps_av[c]
                    p.wait("vector", "sync", rb_ready)
                    p.op("vector", lambda e, av=av, c=c: e.scalar_tensor_tensor(
                        yh[:], av[0:64, :], 1.0, rb64[:, 512 * c: 512 * c + 512],
                        op0=AluOpType.mult, op1=AluOpType.mult))
                    div_done[(par, c)] = p.n["vector"]
                    p.wait("sync", "vector")
                    p.dma("sync",
                          y_sb[64 * (h % 2): 64 * (h % 2) + 64,
                               (h // 2) * T + 512 * c: (h // 2) * T + 512 * c + 512],
                          yh[:])
                    p.wait("vector", "sync")
                stg_free[par] = ("tensor", p.mark("tensor"))

            p.barrier()

        x_sb = sb("x_sb", [128, DC * TP], F32)
        wpr = sb("wpr", [128, DC * D], BF16)
        bpc = sb("bpc", [128, DC], F32)
        ps_pool = [pst("pr_ps0", [128, 512]), pst("pr_ps1", [128, 512])]
        for dc in range(DC):
            p.dma("gpsimd", x_sb[:, dc * TP + PAD: dc * TP + PAD + T],
                  x2_d[dc * 128:(dc + 1) * 128, :])
        p.dma("gpsimd", wpr[:].rearrange("p (c o) -> p c o", c=DC),
              wp_d.rearrange("(c p) o -> p c o", p=128))
        p.dma("gpsimd", bpc[:], bp_d[:])
        p.wait("gpsimd", "gpsimd")
        for e in ("tensor", "vector"):
            p.wait(e, "gpsimd")

        evac_done = {}
        for oc in range(DC):
            for tc in range(NT):
                idx = oc * NT + tc
                ps = ps_pool[idx % 2]
                if idx >= 2:
                    p.wait("tensor", "vector", evac_done[idx - 2])
                for ec in range(DC):
                    p.op("tensor", lambda e, ps=ps, ec=ec, oc=oc, tc=tc: e.matmul(
                        ps[:], wpr[:, ec * D + oc * 128: ec * D + oc * 128 + 128],
                        y_sb[:, ec * T + 512 * tc: ec * T + 512 * tc + 512],
                        start=(ec == 0), stop=(ec == DC - 1)), inc=(ec == DC - 1))
                p.wait("vector", "tensor")
                co = oc * TP + PAD + 512 * tc
                evac_done[idx] = p.op(
                    "vector", lambda e, ps=ps, oc=oc, co=co: e.scalar_tensor_tensor(
                        x_sb[:, co: co + 512], ps[:], bpc[:, oc:oc + 1],
                        x_sb[:, co: co + 512], op0=AluOpType.add, op1=AluOpType.add))

        p.wait("sync", "vector")
        for dc in range(DC):
            p.dma("sync", x3_o[dc * 128:(dc + 1) * 128, :],
                  x_sb[:, dc * TP + PAD: dc * TP + PAD + T])
        p.wait("sync", "sync")
        p.finalize()
    return nc



def build_l5():
    import contextlib
    nc = bass.Bass("TRN2", target_bir_lowering=False)
    xT = nc.dram_tensor("xT", [D, T], F32, kind="ExternalInput").ap()
    halo = nc.dram_tensor("halo", [D, 1], F32, kind="ExternalInput").ap()
    wr_d = nc.dram_tensor("w_rT", [D, D], F32, kind="ExternalInput").ap()
    wk_d = nc.dram_tensor("w_kT", [D, D], F32, kind="ExternalInput").ap()
    wv_d = nc.dram_tensor("w_vT", [D, D], F32, kind="ExternalInput").ap()
    mur_d = nc.dram_tensor("mu_r", [128, DC], F32, kind="ExternalInput").ap()
    muk_d = nc.dram_tensor("mu_k", [128, DC], F32, kind="ExternalInput").ap()
    x4_o = nc.dram_tensor("x4T", [D, T], F32, kind="ExternalOutput").ap()

    with contextlib.ExitStack() as stack:
        sb = lambda name, shp, dt: stack.enter_context(nc.sbuf_tensor(name, shp, dt))
        x_sb = sb("x_sb", [128, DC * TP], F32)
        xn_sb = sb("xn_sb", [128, DC * TP], BF16)
        mur = sb("mur", [128, DC], F32)
        muk = sb("muk", [128, DC], F32)
        scr = ln_scratch(nc, stack)
        p = make_prog(nc, stack)

        emit_const_init(p, scr)
        for dc in range(DC):
            p.dma("gpsimd", x_sb[:, dc * TP + PAD: dc * TP + PAD + T],
                  xT[dc * 128:(dc + 1) * 128, :])
            p.dma("gpsimd", x_sb[:, dc * TP + PAD - 1: dc * TP + PAD],
                  halo[dc * 128:(dc + 1) * 128, :])
        p.dma("gpsimd", mur[:], mur_d[:])
        p.dma("gpsimd", muk[:], muk_d[:])
        p.wait("gpsimd", "gpsimd")
        for e in ("tensor", "vector", "scalar", "sync"):
            p.wait(e, "gpsimd")

        emit_ffn(p, nc, stack, x_sb, xn_sb, scr, wr_d, wk_d, wv_d, mur, muk)

        p.wait("sync", "vector")
        for dc in range(DC):
            p.dma("sync", x4_o[dc * 128:(dc + 1) * 128, :],
                  x_sb[:, dc * TP + PAD: dc * TP + PAD + T])
        p.wait("sync", "sync")
        p.finalize()
    return nc



def _c(x):
    return np.ascontiguousarray(x)


def kernel(**inputs):
    x0 = np.asarray(inputs["L0_attn_a_x"])
    W_in = np.asarray(inputs["L0_attn_a_W_in"])
    b_in = np.asarray(inputs["L0_attn_a_b_in"])
    W_B = np.asarray(inputs["L0_attn_a_W_B"])
    W_C = np.asarray(inputs["L0_attn_a_W_C"])
    D_v = np.asarray(inputs["L0_attn_a_D"])
    W_out = np.asarray(inputs["L0_attn_a_W_out"])
    b_out = np.asarray(inputs["L0_attn_a_b_out"])
    W_kqv = np.asarray(inputs["L0_attn_b_W_kqv"])
    b_kqv = np.asarray(inputs["L0_attn_b_b_kqv"])
    W_proj = np.asarray(inputs["L0_attn_b_W_proj"])
    b_proj = np.asarray(inputs["L0_attn_b_b_proj"])
    slopes = np.asarray(inputs["L0_attn_b_slopes"])
    W_r = np.asarray(inputs["L0_ffn_W_r"])
    W_k = np.asarray(inputs["L0_ffn_W_k"])
    W_v = np.asarray(inputs["L0_ffn_W_v"])
    mu_r = np.asarray(inputs["L0_ffn_mu_r"])
    mu_k = np.asarray(inputs["L0_ffn_mu_k"])

    cores = list(range(NCORES))
    xs = [_c(x0[c // 2, T * (c % 2): T * (c % 2) + T, :].T) for c in cores]

    l1_shared = {
        "w_inT": _c(W_in.T),
        "b_in": _c(b_in.reshape(2 * DI // 128, 128).T),
        "w_bT": _c(W_B.T),
    }
    r1 = run_bass_kernel_spmd(build_l1(), [dict(l1_shared, xT=xs[c]) for c in cores],
                              core_ids=cores).results

    z16 = np.zeros((16, T), np.float32)
    l2_shared = {
        "w_cT": _c(W_C.T), "w_outT": _c(W_out.T),
        "b_out": _c(b_out.reshape(DC, 128).T),
        "d_vec": _c(D_v.reshape(16, 128).T),
    }
    maps2 = []
    for c in cores:
        peer_b = z16 if c % 2 == 0 else np.asarray(r1[c - 1]["b_ssm"])
        maps2.append(dict(
            l2_shared, xT=xs[c],
            xp=np.asarray(r1[c]["xp"]), sz=np.asarray(r1[c]["sz"]),
            b_ext=_c(np.concatenate([peer_b, np.asarray(r1[c]["b_ssm"])], axis=1)),
        ))
    r2 = run_bass_kernel_spmd(build_l2(), maps2, core_ids=cores).results

    zcol = np.zeros((D, 1), np.float32)
    l3_shared = {
        "w_rT": _c(W_r.T), "w_kT": _c(W_k.T), "w_vT": _c(W_v.T),
        "mu_r": _c(mu_r.reshape(DC, 128).T), "mu_k": _c(mu_k.reshape(DC, 128).T),
        "w_kqvT": _c(W_kqv), "b_kqv": _c(b_kqv.reshape(3 * DC, 128).T),
    }
    maps3 = []
    for c in cores:
        halo = zcol if c % 2 == 0 else _c(np.asarray(r2[c - 1]["x1T"])[:, T - 1: T])
        maps3.append(dict(l3_shared, xT=np.asarray(r2[c]["x1T"]), halo=halo))
    r3 = run_bass_kernel_spmd(build_l3(), maps3, core_ids=cores).results

    zkT = np.zeros((D, T), BF)
    zvt = np.zeros((T, D), BF)
    g = np.arange(T, dtype=np.float32)
    hi_own = np.floor(g / 128) * 128
    lo_own = g - hi_own
    g2 = np.arange(2 * T, dtype=np.float32)
    hi_all = np.floor(g2 / 128) * 128
    lo_all = g2 - hi_all
    l4_shared = {
        "slopes_b": _c(np.broadcast_to(slopes[None, :], (128, NH))),
        "w_proj": _c(W_proj), "b_proj": _c(b_proj.reshape(DC, 128).T),
        "ones_row": np.ones((1, 2 * T), np.float32),
        "zero_row": np.zeros((1, T), np.float32),
    }
    maps4 = []
    for c in cores:
        if c % 2 == 0:
            kT_loc = _c(np.concatenate([zkT, np.asarray(r3[c]["kT"])], axis=1))
            vtok = _c(np.vstack([zvt, np.asarray(r3[c]["vT"]).T]))
            khi = _c(np.concatenate([np.full(T, -1e33, np.float32), hi_own])[None, :])
            klo = _c(np.concatenate([np.zeros(T, np.float32), lo_own])[None, :])
        else:
            kT_loc = _c(np.concatenate(
                [np.asarray(r3[c - 1]["kT"]), np.asarray(r3[c]["kT"])], axis=1))
            vtok = _c(np.vstack(
                [np.asarray(r3[c - 1]["vT"]).T, np.asarray(r3[c]["vT"]).T]))
            khi = _c(hi_all[None, :])
            klo = _c(lo_all[None, :])
        maps4.append(dict(
            l4_shared, x2T=np.asarray(r3[c]["x2T"]), kT_loc=kT_loc,
            qT8=np.asarray(r3[c]["qT"]), vtok=vtok, kpos_hi=khi, kpos_lo=klo))
    r4 = run_bass_kernel_spmd(build_l4(), maps4, core_ids=cores).results

    l5_shared = {
        "w_rT": l3_shared["w_rT"], "w_kT": l3_shared["w_kT"],
        "w_vT": l3_shared["w_vT"], "mu_r": l3_shared["mu_r"],
        "mu_k": l3_shared["mu_k"],
    }
    maps5 = []
    for c in cores:
        halo = zcol if c % 2 == 0 else _c(np.asarray(r4[c - 1]["x3T"])[:, T - 1: T])
        maps5.append(dict(l5_shared, xT=np.asarray(r4[c]["x3T"]), halo=halo))
    r5 = run_bass_kernel_spmd(build_l5(), maps5, core_ids=cores).results

    out = np.empty((B, S, D), np.float32)
    for c in cores:
        out[c // 2, T * (c % 2): T * (c % 2) + T, :] = np.asarray(r5[c]["x4T"]).T
    return out
